# version 04
# baseline (speedup 1.0000x reference)
import numpy as np
from contextlib import ExitStack
import concourse.bass as bass
import concourse.mybir as mybir

F32 = mybir.dt.float32
BF16 = mybir.dt.bfloat16
AF = mybir.ActivationFunctionType
ALU = mybir.AluOpType
AX = mybir.AxisListType

D = 2048
TI = 1408
NU = 18
NI = 11
ND = 16
TT = 2048
TH = 1024
NJ = 8
EPS = 1e-5
BIG = 1e30
NEG_BIG = -1e30
RING13 = 3

ENGS = ("tensor", "vector", "scalar", "sync", "gpsimd")


class Mk:
    def __init__(self, nc):
        self.nc = nc
        self.prog = {e: [] for e in ENGS}
        self.cnt = {}
        self.sem = {}
        self.bank_free = {}
        self.bank_ready = {}
        self.emitted_wait = {}
        self.w13_slot_free = [0] * RING13
        self.w2_slot_free = [0] * 3
        self.sil_free = [[0, 0], [0, 0]]
        self.w13_seq = 0
        self.w13_seq2 = 0
        from collections import deque
        self.w13_jobs = deque()
        self.w2_jobs = deque()
        self.w13_loaded = {}
        self.w2_loaded = {}
        self.w2_emit_seq = 0
        self.w2_cons_seq = 0
        self.call_seq = 0

    def new_sem(self, name, es):
        self.sem[name] = es.enter_context(self.nc.semaphore(name))
        self.cnt[name] = 0

    def emit(self, eng, fn):
        self.prog[eng].append(fn)

    def wait(self, eng, name, val):
        if val <= 0:
            return
        key = (eng, name)
        if val <= self.emitted_wait.get(key, -1):
            return
        self.emitted_wait[key] = val
        s = self.sem[name]
        self.emit(eng, lambda e, s=s, v=val: e.wait_ge(s, v))

    def wait_bank_free(self, eng, bank):
        if bank in self.bank_free:
            nm, v = self.bank_free[bank]
            self.wait(eng, nm, v)

    def wait_bank_ready(self, eng, bank):
        nm, v = self.bank_ready[bank]
        self.wait(eng, nm, v)

    def op(self, eng, name, n, fn):
        self.cnt[name] += n
        v = self.cnt[name]
        s = self.sem[name]
        self.emit(eng, lambda e, fn=fn, s=s, n=n: fn(e).then_inc(s, n))
        return v

    def raw(self, eng, fn):
        self.emit(eng, fn)

    def wait_stores(self, eng):
        for nm in ("xsa", "xsb"):
            self.wait(eng, nm, self.cnt[nm])

    def chain(self, eng, name, fn):
        self.wait(eng, name, self.cnt[name])
        return self.op(eng, name, 1, fn)

    def run_block(self):
        nc = self.nc
        with nc.Block() as block:
            @block.tensor
            def _(t):
                for f in self.prog["tensor"]:
                    f(t)

            @block.vector
            def _(v):
                for f in self.prog["vector"]:
                    f(v)

            @block.scalar
            def _(s):
                for f in self.prog["scalar"]:
                    f(s)

            @block.sync
            def _(sy):
                for f in self.prog["sync"]:
                    f(sy)

            @block.gpsimd
            def _(g):
                for f in self.prog["gpsimd"]:
                    f(g)
        self.prog = {e: [] for e in ENGS}


def alloc_ffn_tensors(nc, es, pfx=""):
    def sb(name, shape, dtype, **kw):
        return es.enter_context(nc.sbuf_tensor(pfx + name, shape, dtype, **kw))
    t = {}
    t["x_h"] = sb("x_h", [128, NJ, D], F32)
    t["lnxT"] = sb("lnxT", [128, ND, TH], BF16, is_transpose_dest=True)
    t["hT"] = sb("hT", [128, 2, NI, TH], BF16)
    t["w13r"] = sb("w13r", [128, RING13, 2, ND, 128], BF16)
    t["w2r"] = sb("w2r", [128, NI, 512], BF16)
    t["lnbf"] = sb("lnbf", [128, 2, D], BF16)
    t["sil"] = sb("sil", [128, 2, 2, 512], F32)
    t["scr"] = sb("scr", [128, D], F32)
    t["stat"] = sb("stat", [128, 8, NJ], F32)
    t["sgate"] = sb("sgate", [128, NJ, 16], F32)
    t["cw05"] = sb("cw05", [128, NJ, NU], F32)
    t["gt"] = sb("gt", [128, 8, 16], F32)
    t["gsml"] = sb("gsml", [128, 8], F32)
    return t


def emit_ln_pass(mk, t, x_src_t, half, lnxT, x_h):
    stat, scr, lnbf = t["stat"], t["scr"], t["lnbf"]
    xr = x_src_t.rearrange("(j p) d -> p j d", p=128)

    xload_done = []
    xload_sem = []
    mk.wait_stores("sync")
    for g in range(2):
        j0 = half * NJ + g * 4
        sem = "xa" if g == 0 else "xb"
        v = mk.op("sync", sem, 16, lambda e, g=g, j0=j0: e.dma_start(
            x_h[:, g * 4:(g + 1) * 4, :], xr[:, j0:j0 + 4, :]))
        xload_done.append(v)
        xload_sem.append(sem)

    red_done = []
    for j in range(NJ):
        mk.wait("vector", xload_sem[j // 4], xload_done[j // 4])
        mk.wait("scalar", xload_sem[j // 4], xload_done[j // 4])
        mk.op("vector", "v", 1, lambda e, j=j: e.reduce_sum(
            stat[:, 0, j:j + 1], x_h[:, j, :], axis=AX.X))
        sv = mk.op("scalar", "s", 1, lambda e, j=j: e.activation(
            scr[:], x_h[:, j, :], AF.Square, accum_out=stat[:, 1, j:j + 1]))
        red_done.append(sv)

    mk.wait("vector", "s", red_done[-1])
    mk.chain("vector", "v", lambda e: e.tensor_scalar(
        stat[:, 2, :], stat[:, 0, :], 1.0 / D, None, ALU.mult))
    mk.chain("vector", "v", lambda e: e.tensor_scalar(
        stat[:, 3, :], stat[:, 1, :], 1.0 / D, None, ALU.mult))
    mk.chain("vector", "v", lambda e: e.tensor_tensor(
        stat[:, 4, :], stat[:, 2, :], stat[:, 2, :], ALU.mult))
    vv = mk.chain("vector", "v", lambda e: e.tensor_sub(
        stat[:, 4, :], stat[:, 3, :], stat[:, 4, :]))
    mk.wait("scalar", "v", vv)
    mk.chain("scalar", "s", lambda e: e.activation(
        stat[:, 7, :], stat[:, 4, :], AF.Ln, bias=EPS))
    sv = mk.chain("scalar", "s", lambda e: e.activation(
        stat[:, 5, :], stat[:, 7, :], AF.Exp, scale=-0.5))
    mk.wait("vector", "s", sv)
    mk.chain("vector", "v", lambda e: e.tensor_tensor(
        stat[:, 6, :], stat[:, 2, :], stat[:, 5, :], ALU.mult))
    mk.chain("vector", "v", lambda e: e.tensor_scalar(
        stat[:, 6, :], stat[:, 6, :], -1.0, None, ALU.mult))

    tr_done = []
    for j in range(NJ):
        par = j % 2
        lsem = "lta" if par == 0 else "ltb"
        if j >= 2:
            mk.wait("vector", lsem, tr_done[j - 2])
        emitfn = mk.chain if j == 0 else (lambda eng, nm, f: mk.op(eng, nm, 1, f))
        vv = emitfn("vector", "v", lambda e, j=j, par=par: e.tensor_scalar(
            lnbf[:, par, :], x_h[:, j, :], stat[:, 5, j:j + 1], stat[:, 6, j:j + 1],
            ALU.mult, ALU.add))
        mk.wait("sync", "v", vv)
        tv = mk.op("sync", lsem, 16, lambda e, j=j, par=par: e.dma_start(
            lnxT[:, :, j * 128:(j + 1) * 128], lnbf[:, par, :], transpose=True))
        tr_done.append(tv)
    return tr_done


def emit_gate(mk, t, gwt_s, eb_s, p2, lnxT_tr_done):
    lnxT, sgate, cw05, gt, gsml = t["lnxT"], t["sgate"], t["cw05"], t["gt"], t["gsml"]
    mk.wait("tensor", "lta", mk.cnt["lta"])
    mk.wait("tensor", "ltb", mk.cnt["ltb"])
    sig_done = [0, 0]
    gate_mm_done = [0, 0]
    for j in range(NJ):
        q = j % 2
        pg = p2[:, q * 512:q * 512 + 16]
        if j >= 2:
            mk.wait("tensor", "s", sig_done[q])
        else:
            mk.wait_bank_free("tensor", ("p2", q))
        for d in range(ND):
            if d < ND - 1:
                mk.raw("tensor", lambda e, d=d, j=j, pg=pg: e.matmul(
                    pg, lnxT[:, d, j * 128:(j + 1) * 128], gwt_s[:, d, :],
                    start=(d == 0), stop=False))
            else:
                gate_mm_done[q] = mk.op("tensor", "t", 1, lambda e, d=d, j=j, pg=pg: e.matmul(
                    pg, lnxT[:, d, j * 128:(j + 1) * 128], gwt_s[:, d, :],
                    start=False, stop=True))
        mk.wait("scalar", "t", gate_mm_done[q])
        sig_done[q] = mk.op("scalar", "s", 1, lambda e, j=j, pg=pg: e.activation(
            sgate[:, j, :], pg, AF.Sigmoid))

        mk.wait("vector", "s", sig_done[q])
        mk.chain("vector", "v", lambda e, j=j: e.tensor_tensor(
            gt[:, 0, :], sgate[:, j, :], eb_s[:], ALU.add))
        mk.chain("vector", "v", lambda e: e.reduce_max(
            gsml[:, 0:4], gt[:, 0, :].rearrange("p (a b) -> p a b", a=4), axis=AX.X))
        mk.chain("vector", "v", lambda e: e.reduce_max(
            gsml[:, 4:5], gsml[:, 0:4], axis=AX.X))
        mk.chain("vector", "v", lambda e: e.tensor_scalar(
            gt[:, 7, 0:4], gsml[:, 0:4], gsml[:, 4:5], None, ALU.is_ge))
        mk.chain("vector", "v", lambda e: e.scalar_tensor_tensor(
            gt[:, 7, 4:8], gt[:, 7, 0:4], NEG_BIG, gsml[:, 0:4], ALU.mult, ALU.add))
        mk.chain("vector", "v", lambda e: e.reduce_max(
            gsml[:, 4:5], gt[:, 7, 4:8], axis=AX.X))
        mk.chain("vector", "v", lambda e: e.tensor_scalar(
            gt[:, 7, 0:4], gsml[:, 0:4], gsml[:, 4:5], None, ALU.is_ge))
        mk.chain("vector", "v", lambda e: e.tensor_copy(
            gt[:, 1, :].rearrange("p (a b) -> p a b", a=4),
            gt[:, 7, 0:4].unsqueeze(2).broadcast_to((128, 4, 4))))
        mk.chain("vector", "v", lambda e: e.tensor_tensor(
            gt[:, 2, :], gt[:, 0, :], gt[:, 1, :], ALU.mult))
        mk.chain("vector", "v", lambda e: e.tensor_scalar(
            gt[:, 3, :], gt[:, 1, :], 1.0, None, ALU.subtract))
        mk.chain("vector", "v", lambda e: e.scalar_tensor_tensor(
            gt[:, 3, :], gt[:, 3, :], BIG, gt[:, 2, :], ALU.mult, ALU.add))
        mk.chain("vector", "v", lambda e: e.tensor_copy(gt[:, 4, :], gt[:, 3, :]))
        for k in range(3):
            a = 4 + (k % 2)
            b = 4 + ((k + 1) % 2)
            mk.chain("vector", "v", lambda e, a=a: e.reduce_max(
                gsml[:, 4:5], gt[:, a, :], axis=AX.X))
            mk.chain("vector", "v", lambda e, a=a: e.tensor_scalar(
                gt[:, 6, :], gt[:, a, :], gsml[:, 4:5], None, ALU.is_ge))
            mk.chain("vector", "v", lambda e, a=a, b=b: e.scalar_tensor_tensor(
                gt[:, b, :], gt[:, 6, :], NEG_BIG, gt[:, a, :], ALU.mult, ALU.add))
        mk.chain("vector", "v", lambda e: e.reduce_max(
            gsml[:, 4:5], gt[:, 5, :], axis=AX.X))
        mk.chain("vector", "v", lambda e: e.tensor_scalar(
            gt[:, 6, :], gt[:, 3, :], gsml[:, 4:5], None, ALU.is_ge))
        mk.chain("vector", "v", lambda e, j=j: e.tensor_tensor(
            gt[:, 6, :], sgate[:, j, :], gt[:, 6, :], ALU.mult))
        mk.chain("vector", "v", lambda e: e.reduce_sum(
            gsml[:, 5:6], gt[:, 6, :], axis=AX.X))
        mk.chain("vector", "v", lambda e: e.reciprocal(gsml[:, 6:7], gsml[:, 5:6]))
        mk.chain("vector", "v", lambda e: e.tensor_scalar(
            gsml[:, 6:7], gsml[:, 6:7], 0.5, None, ALU.mult))
        mk.chain("vector", "v", lambda e, j=j: e.tensor_scalar(
            cw05[:, j, 0:16], gt[:, 6, :], gsml[:, 6:7], None, ALU.mult))
    gv = mk.op("vector", "v", 1, lambda e: e.memset(cw05[:, :, 16:18], 0.5))
    mk.bank_free[("p2", 0)] = ("s", sig_done[0])
    mk.bank_free[("p2", 1)] = ("s", sig_done[1])
    return gv


def emit_ffn_units(mk, t, W13, W2, p1, p2, half, cw_ready, x_dst_t, first_half_of_program):
    x_h, lnxT, hT, w13r, w2r, sil, cw05 = (
        t["x_h"], t["lnxT"], t["hT"], t["w13r"], t["w2r"], t["sil"], t["cw05"])

    W13r = [[W13[u, m] for m in range(2)] for u in range(NU)]

    mk.call_seq += 1
    cs = mk.call_seq
    for u in range(NU):
        for i in range(NI):
            mk.w13_jobs.append((cs, u, i, W13r))
        for dq in range(4):
            for bi, (s0, bn) in enumerate(((0, 4), (4, 4), (8, 3))):
                mk.w2_jobs.append((cs, u, dq, bi, s0, bn, W2))

    def pump_w13():
        if not mk.w13_jobs:
            return
        jcs, u, i, Wr = mk.w13_jobs.popleft()
        slot = mk.w13_seq % RING13
        mk.w13_seq += 1
        mk.wait("gpsimd", "t", mk.w13_slot_free[slot])
        for m in range(2):
            v = mk.op("gpsimd", "wr%d" % slot, 16, lambda e, u=u, m=m, i=i, slot=slot, Wr=Wr: e.dma_start(
                w13r[:, slot, m], Wr[u][m][i]))
        mk.w13_loaded[(jcs, u, i)] = (slot, v)

    def pump_w2():
        if not mk.w2_jobs:
            return
        jcs, u, dq, bi, s0, bn, W2a = mk.w2_jobs.popleft()
        mk.wait("gpsimd", "t", mk.w2_slot_free[bi])
        v = mk.op("gpsimd", "wb%d" % bi, 16, lambda e, u=u, dq=dq, s0=s0, bn=bn, W2a=W2a: e.dma_start(
            w2r[:, s0:s0 + bn, :], W2a[u, dq, :, s0:s0 + bn, :]))
        mk.w2_loaded[(jcs, u, dq, bi)] = v
        mk.w2_emit_seq += 1

    while mk.w13_seq < mk.w13_seq2 + RING13 and mk.w13_jobs:
        pump_w13()
    while mk.w2_emit_seq < mk.w2_cons_seq + 3 and mk.w2_jobs:
        pump_w2()

    hT_ready = {}
    sil_free = mk.sil_free
    unit_final_stt = {}

    cw_waited = [False]

    for u in range(NU):
        ubuf = u % 2
        for i in range(NI):
            slot = mk.w13_seq2 % RING13
            mk.w13_seq2 += 1
            _wslot, _wval = mk.w13_loaded[(cs, u, i)]
            mk.wait("tensor", "wr%d" % _wslot, _wval)
            par = i % 2
            last_mm = 0
            for m in range(2):
                for t2 in range(2):
                    bank = ("p1", m * 2 + t2)
                    mk.wait_bank_free("tensor", bank)
                    for d in range(ND):
                        if d < ND - 1:
                            mk.raw("tensor", lambda e, slot=slot, m=m, t2=t2, d=d: e.matmul(
                                p1[:, (m * 2 + t2) * 512:(m * 2 + t2 + 1) * 512],
                                w13r[:, slot, m, d, :], lnxT[:, d, t2 * 512:(t2 + 1) * 512],
                                start=(d == 0), stop=False))
                        else:
                            last_mm = mk.op("tensor", "t", 1, lambda e, slot=slot, m=m, t2=t2, d=d: e.matmul(
                                p1[:, (m * 2 + t2) * 512:(m * 2 + t2 + 1) * 512],
                                w13r[:, slot, m, d, :], lnxT[:, d, t2 * 512:(t2 + 1) * 512],
                                start=False, stop=True))
                    mk.bank_ready[bank] = ("t", last_mm)
            mk.w13_slot_free[slot] = last_mm
            pump_w13()

            for t2 in range(2):
                mk.wait_bank_ready("scalar", ("p1", t2))
                mk.wait("scalar", "v", sil_free[par][t2])
                sv = mk.op("scalar", "s", 1, lambda e, par=par, t2=t2: e.activation(
                    sil[:, par, t2, :], p1[:, t2 * 512:(t2 + 1) * 512], AF.Silu))
                mk.bank_free[("p1", t2)] = ("s", sv)
                mk.wait_bank_ready("vector", ("p1", 2 + t2))
                mk.wait("vector", "s", sv)
                vv = mk.op("vector", "v", 1, lambda e, ubuf=ubuf, i=i, par=par, t2=t2: e.tensor_tensor(
                    hT[:, ubuf, i, t2 * 512:(t2 + 1) * 512], sil[:, par, t2, :],
                    p1[:, (2 + t2) * 512:(3 + t2) * 512], ALU.mult))
                mk.bank_free[("p1", 2 + t2)] = ("v", vv)
                sil_free[par][t2] = vv
            hT_ready[(u, i)] = vv

        if not cw_waited[0]:
            mk.wait("vector", "v", cw_ready)
            cw_waited[0] = True
        sub = 0
        for dq in range(4):
            for jg in range(2):
                pset, pname = (p2, "p2") if sub % 2 == 0 else (p1, "p1")
                sub += 1
                last_mm = 0
                for jj in range(4):
                    mk.wait_bank_free("tensor", (pname, jj))
                for i in range(NI):
                    bi = 0 if i < 4 else (1 if i < 8 else 2)
                    mk.wait("tensor", "wb%d" % bi, mk.w2_loaded[(cs, u, dq, bi)])
                    if dq == 0 and jg == 0:
                        mk.wait("tensor", "v", hT_ready[(u, i)])
                    boundary = (jg == 1 and i in (3, 7, 10))
                    for jj in range(4):
                        j = jg * 4 + jj
                        def _mm(e, pset=pset, ubuf=ubuf, i=i, j=j, jj=jj):
                            return e.matmul(
                                pset[:, jj * 512:(jj + 1) * 512],
                                hT[:, ubuf, i, j * 128:(j + 1) * 128], w2r[:, i, :],
                                start=(i == 0), stop=(i == NI - 1))
                        if jj == 3 and (boundary or i == NI - 1):
                            lm = mk.op("tensor", "t", 1, _mm)
                            if i == NI - 1:
                                last_mm = lm
                        else:
                            mk.raw("tensor", _mm)
                    if boundary:
                        mk.w2_slot_free[{3: 0, 7: 1, 10: 2}[i]] = lm
                        mk.w2_cons_seq += 1
                        pump_w2()
                for jj in range(4):
                    j = jg * 4 + jj
                    mk.wait("vector", "t", last_mm)
                    vv = mk.op("vector", "v", 1, lambda e, pset=pset, jj=jj, j=j, dq=dq, u=u: e.scalar_tensor_tensor(
                        x_h[:, j, dq * 512:(dq + 1) * 512], pset[:, jj * 512:(jj + 1) * 512],
                        cw05[:, j, u:u + 1], x_h[:, j, dq * 512:(dq + 1) * 512],
                        ALU.mult, ALU.add))
                    mk.bank_free[(pname, jj)] = ("v", vv)
                    if u == NU - 1 and dq == 3:
                        unit_final_stt[jg] = vv

    xdr = x_dst_t.rearrange("(j p) d -> p j d", p=128)
    store_done = []
    for jg in range(2):
        sem = "xsa" if jg == 0 else "xsb"
        mk.wait("sync", "v", unit_final_stt[jg])
        v = mk.op("sync", sem, 16, lambda e, jg=jg, half=half: e.dma_start(
            xdr[:, half * NJ + jg * 4: half * NJ + jg * 4 + 4, :],
            x_h[:, jg * 4:(jg + 1) * 4, :]))
        store_done.append(v)
    return store_done


def emit_ffn_call(mk, t, x_src_t, x_dst_t, W13, W2, gwt_s, eb_s, p1, p2):
    stores = []
    for half in range(2):
        tr_done = emit_ln_pass(mk, t, x_src_t, half, t["lnxT"], t["x_h"])
        cw_ready = emit_gate(mk, t, gwt_s, eb_s, p2, tr_done)
        st = emit_ffn_units(mk, t, W13, W2, p1, p2, half, cw_ready, x_dst_t, half == 0)
        stores += st
    return stores



NT = 16


def alloc_attn_tensors(nc, es):
    a = {}
    a["ln2T"] = es.enter_context(nc.sbuf_tensor("ln2T", [128, ND, TT], BF16, is_transpose_dest=True))
    a["wq"] = es.enter_context(nc.sbuf_tensor("wq", [128, ND, 1024], BF16))
    a["xtmp"] = es.enter_context(nc.sbuf_tensor("xtmp", [128, 2, D], F32))
    a["lnbf2"] = es.enter_context(nc.sbuf_tensor("lnbf2", [128, 2, D], BF16))
    a["stat2"] = es.enter_context(nc.sbuf_tensor("stat2", [128, 8, NT], F32))
    a["scr2"] = es.enter_context(nc.sbuf_tensor("scr2", [128, 1024], F32))
    a["H"] = es.enter_context(nc.sbuf_tensor("H", [128, 1024], F32))
    a["Ts"] = es.enter_context(nc.sbuf_tensor("Ts", [128, 1024], F32))
    a["xn"] = es.enter_context(nc.sbuf_tensor("xn", [128, 1024], F32))
    a["xaf"] = es.enter_context(nc.sbuf_tensor("xaf", [128, 1024], F32))
    a["xnb"] = es.enter_context(nc.sbuf_tensor("xnb", [128, 2, 1024], BF16))
    a["xnT"] = es.enter_context(nc.sbuf_tensor("xnT", [128, 2, 8, 128], BF16, is_transpose_dest=True))
    a["xstr"] = es.enter_context(nc.sbuf_tensor("xstr", [128, 2, 8, 128], F32))
    a["gst"] = es.enter_context(nc.sbuf_tensor("gst", [128, 8, 16], F32))
    a["rtmp"] = es.enter_context(nc.sbuf_tensor("rtmp", [128, 3, 64], F32))
    a["hhb"] = es.enter_context(nc.sbuf_tensor("hhb", [128, D], F32))
    a["gnw"] = es.enter_context(nc.sbuf_tensor("gnw", [128, D], F32))
    a["gnb"] = es.enter_context(nc.sbuf_tensor("gnb", [128, D], F32))
    a["cosb"] = es.enter_context(nc.sbuf_tensor("cosb", [128, NT, 64], F32))
    a["sinb"] = es.enter_context(nc.sbuf_tensor("sinb", [128, NT, 64], F32))
    return a


def emit_attn(mk, a, x_src_t, x_dst_t, WQ, SMA, p1):
    ln2T, wq, xtmp, lnbf2, stat2, scr2 = a["ln2T"], a["wq"], a["xtmp"], a["lnbf2"], a["stat2"], a["scr2"]
    H, Ts, xn, xaf, xnb, xnT, xstr, gst, rtmp = (
        a["H"], a["Ts"], a["xn"], a["xaf"], a["xnb"], a["xnT"], a["xstr"], a["gst"], a["rtmp"])

    sm_done = mk.op("gpsimd", "pre", 16, lambda e: e.dma_start(a["hhb"][:], SMA[:, 0:D]))
    sm_done = mk.op("gpsimd", "pre", 16, lambda e: e.dma_start(a["gnw"][:], SMA[:, D:2 * D]))
    sm_done = mk.op("gpsimd", "pre", 16, lambda e: e.dma_start(a["gnb"][:], SMA[:, 2 * D:3 * D]))
    sm_done = mk.op("gpsimd", "pre", 16, lambda e: e.dma_start(
        a["cosb"][:], SMA[:, 3 * D:3 * D + NT * 64].rearrange("p (j c) -> p j c", c=64)))
    sm_done = mk.op("gpsimd", "pre", 16, lambda e: e.dma_start(
        a["sinb"][:], SMA[:, 3 * D + NT * 64:3 * D + 2 * NT * 64].rearrange("p (j c) -> p j c", c=64)))

    xr = x_src_t.rearrange("(j p) d -> p j d", p=128)
    xdr = x_dst_t.rearrange("(j p) d -> p j d", p=128)

    norm_done = [0] * NT
    red_done = []
    redv_done = [0] * NT
    reds_done = [0] * NT
    xload = [0] * NT
    for j in range(NT):
        par = j % 2
        xsem = "xa" if par == 0 else "xb"
        if j >= 2:
            mk.wait("sync", "v", redv_done[j - 2])
            mk.wait("sync", "s", reds_done[j - 2])
        xload[j] = mk.op("sync", xsem, 16, lambda e, j=j, par=par: e.dma_start(
            xtmp[:, par, :], xr[:, j, :]))
        mk.wait("vector", xsem, xload[j])
        mk.wait("scalar", xsem, xload[j])
        redv_done[j] = mk.op("vector", "v", 1, lambda e, j=j, par=par: e.reduce_sum(
            stat2[:, 0, j:j + 1], xtmp[:, par, :], axis=AX.X))
        sv = mk.op("scalar", "s", 1, lambda e, j=j, par=par: e.activation(
            scr2[:], xtmp[:, par, 0:1024], AF.Square, accum_out=stat2[:, 2, j:j + 1]))
        sv = mk.op("scalar", "s", 1, lambda e, j=j, par=par: e.activation(
            scr2[:], xtmp[:, par, 1024:2048], AF.Square, accum_out=stat2[:, 3, j:j + 1]))
        reds_done[j] = sv
        red_done.append(sv)
    mk.wait("vector", "s", red_done[-1])
    mk.chain("vector", "v", lambda e: e.tensor_add(
        stat2[:, 1, :], stat2[:, 2, :], stat2[:, 3, :]))
    mk.chain("vector", "v", lambda e: e.tensor_scalar(
        stat2[:, 2, :], stat2[:, 0, :], 1.0 / D, None, ALU.mult))
    mk.chain("vector", "v", lambda e: e.tensor_scalar(
        stat2[:, 3, :], stat2[:, 1, :], 1.0 / D, None, ALU.mult))
    mk.chain("vector", "v", lambda e: e.tensor_tensor(
        stat2[:, 4, :], stat2[:, 2, :], stat2[:, 2, :], ALU.mult))
    vv = mk.chain("vector", "v", lambda e: e.tensor_sub(
        stat2[:, 4, :], stat2[:, 3, :], stat2[:, 4, :]))
    mk.wait("scalar", "v", vv)
    mk.chain("scalar", "s", lambda e: e.activation(
        stat2[:, 7, :], stat2[:, 4, :], AF.Ln, bias=EPS))
    sv = mk.chain("scalar", "s", lambda e: e.activation(
        stat2[:, 5, :], stat2[:, 7, :], AF.Exp, scale=-0.5))
    mk.wait("vector", "s", sv)
    mk.chain("vector", "v", lambda e: e.tensor_tensor(
        stat2[:, 6, :], stat2[:, 2, :], stat2[:, 5, :], ALU.mult))
    mk.chain("vector", "v", lambda e: e.tensor_scalar(
        stat2[:, 6, :], stat2[:, 6, :], -1.0, None, ALU.mult))
    tr_done = [0] * NT
    for j in range(NT):
        par = j % 2
        xsem = "xa" if par == 0 else "xb"
        lsem = "lta" if par == 0 else "ltb"
        if j < 2:
            mk.wait("sync", "v", redv_done[NT - 2 + j])
            mk.wait("sync", "s", reds_done[NT - 2 + j])
        else:
            mk.wait("sync", "v", norm_done[j - 2])
        ld = mk.op("sync", xsem, 16, lambda e, j=j, par=par: e.dma_start(
            xtmp[:, par, :], xr[:, j, :]))
        mk.wait("vector", xsem, ld)
        if j >= 2:
            mk.wait("vector", lsem, tr_done[j - 2])
        emitfn = mk.chain if j == 0 else (lambda eng, nm, f: mk.op(eng, nm, 1, f))
        norm_done[j] = emitfn("vector", "v", lambda e, j=j, par=par: e.tensor_scalar(
            lnbf2[:, par, :], xtmp[:, par, :], stat2[:, 5, j:j + 1], stat2[:, 6, j:j + 1],
            ALU.mult, ALU.add))
        mk.wait("sync", "v", norm_done[j])
        tr_done[j] = mk.op("sync", lsem, 16, lambda e, j=j, par=par: e.dma_start(
            ln2T[:, :, j * 128:(j + 1) * 128], lnbf2[:, par, :], transpose=True))

    relu_done = {}
    stripe_ops = {}
    last_store = 0
    tanh_prev = [0]
    ts_lastread = [0]
    mk.wait("vector", "pre", sm_done)
    qmm = 0
    for ch in range(2):
        mk.wait("gpsimd", "t", qmm)
        wq_done = mk.op("gpsimd", "wq", 16, lambda e, ch=ch: e.dma_start(
            wq[:], WQ[:, ch * 1024:(ch + 1) * 1024].rearrange("(k p) c -> p k c", p=128)))
        if ch == 0:
            mk.wait("tensor", "lta", mk.cnt["lta"])
            mk.wait("tensor", "ltb", mk.cnt["ltb"])
        mk.wait("tensor", "wq", wq_done)
        for j in range(NT):
            seq = ch * NT + j
            par = j % 2
            pq = p1[:, par * 1024:(par + 1) * 1024]
            bank = ("pq", par)
            mk.wait_bank_free("tensor", bank)
            for d in range(ND):
                for c2 in range(2):
                    fn = lambda e, d=d, c2=c2, j=j, pq=pq: e.matmul(
                        pq[:, c2 * 512:(c2 + 1) * 512], ln2T[:, d, j * 128:(j + 1) * 128],
                        wq[:, d, c2 * 512:(c2 + 1) * 512],
                        start=(d == 0), stop=(d == ND - 1))
                    if d == ND - 1 and c2 == 1:
                        qmm = mk.op("tensor", "t", 1, fn)
                    else:
                        mk.raw("tensor", fn)
            mk.bank_ready[bank] = ("t", qmm)

            mk.wait("vector", "t", qmm)
            if ch == 0:
                mk.chain("vector", "v", lambda e, pq=pq: e.tensor_scalar(
                    rtmp[:, 0, 0:64:2], pq[:, 1:64:2], -1.0, None, ALU.mult))
                mk.chain("vector", "v", lambda e, pq=pq: e.tensor_copy(
                    rtmp[:, 0, 1:64:2], pq[:, 0:64:2]))
                mk.chain("vector", "v", lambda e, j=j: e.tensor_tensor(
                    rtmp[:, 1, :], rtmp[:, 0, :], a["sinb"][:, j, :], ALU.mult))
                mk.chain("vector", "v", lambda e, pq=pq, j=j: e.tensor_tensor(
                    rtmp[:, 2, :], pq[:, 0:64], a["cosb"][:, j, :], ALU.mult))
                mk.chain("vector", "v", lambda e, pq=pq: e.tensor_add(
                    pq[:, 0:64], rtmp[:, 1, :], rtmp[:, 2, :]))
            mk.wait("vector", "s", tanh_prev[0])
            rd = mk.chain("vector", "v", lambda e, pq=pq, ch=ch: e.scalar_tensor_tensor(
                H[:], pq[:], 0.0, a["hhb"][:, ch * 1024:(ch + 1) * 1024], ALU.max, ALU.add))
            mk.bank_free[("pq", par)] = ("v", rd)
            mk.wait("scalar", "v", rd)
            mk.wait("scalar", "v", ts_lastread[0])
            tv = mk.chain("scalar", "s", lambda e: e.activation(Ts[:], H[:], AF.Tanh))
            tanh_prev[0] = tv
            sq = mk.chain("scalar", "s", lambda e: e.activation(scr2[:], Ts[:], AF.Square))
            mk.wait("vector", "s", sq)
            mk.chain("vector", "v", lambda e: e.reduce_sum(
                gst[:, 0, :], Ts[:].rearrange("p (g c) -> p g c", c=64), axis=AX.X))
            mk.chain("vector", "v", lambda e: e.reduce_sum(
                gst[:, 1, :], scr2[:].rearrange("p (g c) -> p g c", c=64), axis=AX.X))
            mk.chain("vector", "v", lambda e: e.tensor_scalar(
                gst[:, 2, :], gst[:, 0, :], 1.0 / 64, None, ALU.mult))
            mk.chain("vector", "v", lambda e: e.tensor_scalar(
                gst[:, 3, :], gst[:, 1, :], 1.0 / 64, None, ALU.mult))
            mk.chain("vector", "v", lambda e: e.tensor_tensor(
                gst[:, 4, :], gst[:, 2, :], gst[:, 2, :], ALU.mult))
            gv = mk.chain("vector", "v", lambda e: e.tensor_sub(
                gst[:, 4, :], gst[:, 3, :], gst[:, 4, :]))
            mk.wait("scalar", "v", gv)
            mk.chain("scalar", "s", lambda e: e.activation(
                gst[:, 7, :], gst[:, 4, :], AF.Ln, bias=EPS))
            gs = mk.chain("scalar", "s", lambda e: e.activation(
                gst[:, 5, :], gst[:, 7, :], AF.Exp, scale=-0.5))
            mk.wait("vector", "s", gs)
            mk.chain("vector", "v", lambda e: e.tensor_tensor(
                gst[:, 6, :], gst[:, 2, :], gst[:, 5, :], ALU.mult))
            mk.chain("vector", "v", lambda e: e.tensor_scalar(
                gst[:, 6, :], gst[:, 6, :], -1.0, None, ALU.mult))
            for g in range(16):
                emitfn = mk.chain if g == 0 else (lambda eng, nm, f: mk.op(eng, nm, 1, f))
                nv = emitfn("vector", "v", lambda e, g=g: e.tensor_scalar(
                    xn[:, g * 64:(g + 1) * 64], Ts[:, g * 64:(g + 1) * 64],
                    gst[:, 5, g:g + 1], gst[:, 6, g:g + 1], ALU.mult, ALU.add))
            ts_lastread[0] = nv
            mk.chain("vector", "v", lambda e, ch=ch: e.tensor_tensor(
                xaf[:], xn[:], a["gnw"][:, ch * 1024:(ch + 1) * 1024], ALU.mult))
            if seq - 2 in stripe_ops:
                mk.wait("vector", "lta" if par == 0 else "ltb", stripe_ops[seq - 2][0])
            xv = mk.chain("vector", "v", lambda e, ch=ch, par=par: e.tensor_tensor(
                xnb[:, par, :], xaf[:], a["gnb"][:, ch * 1024:(ch + 1) * 1024], ALU.add))
            mk.wait("sync", "v", xv)
            if seq - 2 in stripe_ops:
                mk.wait("sync", "v", stripe_ops[seq - 2][2])
            trv = mk.op("sync", "lta" if par == 0 else "ltb", 16, lambda e, par=par: e.dma_start(
                xnT[:, par], xnb[:, par, :], transpose=True))
            if seq - 2 in stripe_ops:
                mk.wait("sync", "xsa" if par == 0 else "xsb", stripe_ops[seq - 2][1])
            sld = mk.op("sync", "xa" if par == 0 else "xb", 16, lambda e, ch=ch, j=j, par=par: e.dma_start(
                xstr[:, par], xr[:, ch * 8:(ch + 1) * 8, j * 128:(j + 1) * 128]))
            mk.wait("vector", "lta" if par == 0 else "ltb", trv)
            mk.wait("vector", "xa" if par == 0 else "xb", sld)
            av = mk.chain("vector", "v", lambda e, par=par: e.tensor_add(
                xstr[:, par].rearrange("p k t -> p (k t)"),
                xstr[:, par].rearrange("p k t -> p (k t)"),
                xnT[:, par].rearrange("p k t -> p (k t)")))
            mk.wait("sync", "v", av)
            stv = mk.op("sync", "xsa" if par == 0 else "xsb", 16, lambda e, ch=ch, j=j, par=par: e.dma_start(
                xdr[:, ch * 8:(ch + 1) * 8, j * 128:(j + 1) * 128], xstr[:, par]))
            stripe_ops[seq] = (trv, stv, av)
            last_store = stv
    return last_store



from concourse.bass_utils import run_bass_kernel_spmd

NCORES = 8
ROPE_DIM = 64
_CACHE = {}


def _rope_tables():
    inv_freq = (1.0 / 10000.0 ** (np.arange(0, ROPE_DIM, 2, dtype=np.float32) / ROPE_DIM)).astype(np.float32)
    freqs = np.arange(TT, dtype=np.float32)[:, None] * inv_freq[None, :]
    freqs = np.repeat(freqs, 2, axis=-1)
    return np.cos(freqs).astype(np.float32), np.sin(freqs).astype(np.float32)


def build_program():
    nc = bass.Bass("TRN2", target_bir_lowering=False)
    x0 = nc.dram_tensor("x0", [TT, D], F32, kind="ExternalInput").ap()
    W13t = nc.dram_tensor("w13", [NU, 2, NI, 128, ND, 128], F32, kind="ExternalInput").ap()
    W2t = nc.dram_tensor("w2", [NU, 4, 128, NI, 512], F32, kind="ExternalInput").ap()
    GWTt = nc.dram_tensor("gwt", [D, 16], F32, kind="ExternalInput").ap()
    EBt = nc.dram_tensor("eb", [128, 16], F32, kind="ExternalInput").ap()
    WQt = nc.dram_tensor("wqd", [D, D], F32, kind="ExternalInput").ap()
    SMAt = nc.dram_tensor("sma", [128, 3 * D + 2 * NT * 64], F32, kind="ExternalInput").ap()
    Ct = nc.dram_tensor("consts", [128, 2], F32, kind="ExternalInput").ap()
    xm1 = nc.dram_tensor("xm1", [TT, D], F32, kind="Internal").ap()
    xm2 = nc.dram_tensor("xm2", [TT, D], F32, kind="Internal").ap()
    xo = nc.dram_tensor("xo", [TT, D], F32, kind="ExternalOutput").ap()

    mk = Mk(nc)
    with ExitStack() as es:
        for s in ("xa", "xb", "xsa", "xsb", "lta", "ltb", "wr0", "wr1", "wr2",
                  "wb0", "wb1", "wb2", "pre", "wq", "t", "v", "s"):
            mk.new_sem(s, es)
        gwt_s = es.enter_context(nc.sbuf_tensor("gwt_s", [128, ND, 16], BF16))
        eb_s = es.enter_context(nc.sbuf_tensor("eb_s", [128, 16], F32))
        cp = es.enter_context(nc.sbuf_tensor("cp", [128, 2], F32))
        p1 = es.enter_context(nc.psum_tensor("p1", [128, 2048], F32))
        p2 = es.enter_context(nc.psum_tensor("p2", [128, 2048], F32))
        nc.const_aps.register(F32, EPS, cp[:, 0:1])
        nc.const_aps.register(F32, -0.5, cp[:, 1:2])

        with ExitStack() as es1:
            t = alloc_ffn_tensors(nc, es1, "f1")
            mk.op("gpsimd", "pre", 16, lambda e: e.dma_start(
                gwt_s[:], GWTt.rearrange("(k p) e -> p k e", p=128)))
            mk.op("gpsimd", "pre", 16, lambda e: e.dma_start(eb_s[:], EBt))
            pre = mk.op("gpsimd", "pre", 16, lambda e: e.dma_start(cp[:], Ct))
            mk.wait("tensor", "pre", pre)
            mk.wait("vector", "pre", pre)
            mk.wait("scalar", "pre", pre)
            emit_ffn_call(mk, t, x0, xm1, W13t, W2t, gwt_s, eb_s, p1, p2)
            mk.run_block()

        with ExitStack() as es2:
            a = alloc_attn_tensors(nc, es2)
            mk.wait_stores("sync")
            mk.wait_stores("gpsimd")
            emit_attn(mk, a, xm1, xm2, WQt, SMAt, p1)
            mk.run_block()

        with ExitStack() as es3:
            t2 = alloc_ffn_tensors(nc, es3, "f2")
            mk.wait_stores("sync")
            mk.wait_stores("gpsimd")
            emit_ffn_call(mk, t2, xm2, xo, W13t, W2t, gwt_s, eb_s, p1, p2)
            mk.wait_stores("sync")
            mk.run_block()
    return nc


def _prep_weights(inp):
    w1, w3, w2 = inp["L0_ffn_w1"], inp["L0_ffn_w3"], inp["L0_ffn_w2"]
    sw1, sw3, sw2 = inp["L0_ffn_sw1"], inp["L0_ffn_sw3"], inp["L0_ffn_sw2"]
    w1u = np.concatenate([w1, sw1.reshape(D, 2, TI).transpose(1, 0, 2)], 0)
    w3u = np.concatenate([w3, sw3.reshape(D, 2, TI).transpose(1, 0, 2)], 0)
    w2u = np.concatenate([w2, sw2.reshape(2, TI, D)], 0)
    W13 = np.ascontiguousarray(np.stack([
        w1u.reshape(NU, ND, 128, NI, 128).transpose(0, 3, 2, 1, 4),
        w3u.reshape(NU, ND, 128, NI, 128).transpose(0, 3, 2, 1, 4)], axis=1))
    W2 = np.ascontiguousarray(w2u.reshape(NU, NI, 128, 4, 512).transpose(0, 3, 2, 1, 4))
    GWT = np.ascontiguousarray(inp["L0_ffn_gate_w"].T)
    EB = np.ascontiguousarray(np.tile(inp["L0_ffn_expert_bias"][None, :], (128, 1)))
    WQ = np.ascontiguousarray(inp["L0_attn_W_Q"])
    cos, sin = _rope_tables()
    cosl = cos.reshape(NT, 128, 64).transpose(1, 0, 2).reshape(128, NT * 64)
    sinl = sin.reshape(NT, 128, 64).transpose(1, 0, 2).reshape(128, NT * 64)
    SMA = np.ascontiguousarray(np.concatenate([
        np.tile(inp["L0_attn_hh_b"][None, :], (128, 1)),
        np.tile(inp["L0_attn_gn_weight"][None, :], (128, 1)),
        np.tile(inp["L0_attn_gn_bias"][None, :], (128, 1)),
        cosl, sinl], axis=1)).astype(np.float32)
    CONSTS = np.tile(np.array([[EPS, -0.5]], np.float32), (128, 1))
    return W13, W2, GWT, EB, WQ, SMA, CONSTS


def kernel(**inp):
    _ = inp.get("L0_attn_S_n")
    X = inp["L0_attn_X"]
    W13, W2, GWT, EB, WQ, SMA, CONSTS = _prep_weights(inp)
    if "nc" not in _CACHE:
        _CACHE["nc"] = build_program()
    nc = _CACHE["nc"]
    in_maps = [dict(x0=np.ascontiguousarray(X[c]), w13=W13, w2=W2, gwt=GWT, eb=EB,
                    wqd=WQ, sma=SMA, consts=CONSTS) for c in range(NCORES)]
    res = run_bass_kernel_spmd(nc, in_maps, core_ids=list(range(NCORES)))
    return np.stack([np.asarray(res.results[c]["xo"], dtype=np.float32) for c in range(NCORES)], axis=0)
